# version 02 (final)
# baseline (speedup 1.0000x reference)
import numpy as np
from contextlib import ExitStack
import ml_dtypes
import concourse.bass as bass
import concourse.mybir as mybir
from concourse.bass_utils import run_bass_kernel_spmd

F32 = mybir.dt.float32
BF = mybir.dt.bfloat16
I32 = mybir.dt.int32
AF = mybir.ActivationFunctionType
OP = mybir.AluOpType
AX = mybir.AxisListType

NB = 4
T = 2048
D = 1024
NH = 16
HD = 64
NQT = 4
BFNP = ml_dtypes.bfloat16


class Prog:
    def __init__(self, nc):
        self.nc = nc
        self.ops = []

    def mm(self, out, l, r, start=True, stop=True):
        nc = self.nc
        self.ops.append(("tensor",
                         lambda o=out, a=l, b=r, s=start, e=stop:
                         nc.tensor.matmul(o, a, b, start=s, stop=e), 1))

    def act(self, out, in_, f, bias=None, scale=None):
        nc = self.nc
        kw = {}
        if bias is not None:
            kw["bias"] = bias
        if scale is not None:
            kw["scale"] = scale
        self.ops.append(("scalar",
                         lambda o=out, i=in_, fn=f, k=dict(kw):
                         nc.scalar.activation(o, i, fn, **k), 1))

    def tt(self, out, a, b, op):
        nc = self.nc
        self.ops.append(("vector",
                         lambda o=out, x=a, y=b, p=op:
                         nc.vector.tensor_tensor(o, x, y, p), 1))

    def ts1(self, out, a, s1, op):
        nc = self.nc
        self.ops.append(("vector",
                         lambda o=out, x=a, s=s1, p=op:
                         nc.vector.tensor_scalar(o, x, s, None, p), 1))

    def ts2(self, out, a, s1, s2, op0, op1):
        nc = self.nc
        self.ops.append(("vector",
                         lambda o=out, x=a, u=s1, v=s2, p=op0, q=op1:
                         nc.vector.tensor_scalar(o, x, u, v, p, q), 1))

    def cp(self, out, in_):
        nc = self.nc
        self.ops.append(("vector",
                         lambda o=out, i=in_: nc.vector.tensor_copy(o, i), 1))

    def recip(self, out, in_):
        nc = self.nc
        self.ops.append(("vector",
                         lambda o=out, i=in_: nc.vector.reciprocal(o, i), 1))

    def memset(self, ap, v):
        nc = self.nc
        self.ops.append(("vector",
                         lambda a=ap, x=v: nc.vector.memset(a, x), 1))

    def red(self, out, in_):
        nc = self.nc
        self.ops.append(("vector",
                         lambda o=out, i=in_:
                         nc.vector.reduce_sum(o, i, AX.X), 1))

    def dma(self, dst, src, slow=False):
        nc = self.nc
        kw = {}
        if slow:
            kw["allow_slow_non_contiguous"] = True
        self.ops.append(("sync",
                         lambda d=dst, s=src, k=dict(kw):
                         nc.sync.dma_start(d, s, **k), 16))

    def finalize(self, st):
        nc = self.nc
        CH = 20000
        meta = []
        chunk_tot = []
        cur = 0
        cum = 0
        for (eng, fn, inc) in self.ops:
            if cum + inc > CH:
                chunk_tot.append(cum)
                cur += 1
                cum = 0
            meta.append((eng, fn, inc, cur, cum))
            cum += inc
        chunk_tot.append(cum)
        sems = [st.enter_context(nc.semaphore("sem%d" % i))
                for i in range(len(chunk_tot))]
        blk = st.enter_context(nc.Block())

        def mk_section(ename):
            def body(eng):
                seen = set()
                for (e, fn, inc, ch, wv) in meta:
                    if e != ename:
                        continue
                    if ch not in seen:
                        seen.add(ch)
                        if ch > 0:
                            eng.wait_ge(sems[ch - 1], chunk_tot[ch - 1])
                    eng.wait_ge(sems[ch], wv)
                    fn().then_inc(sems[ch], inc)
                if len(chunk_tot) > 1:
                    eng.wait_ge(sems[len(chunk_tot) - 2], chunk_tot[-2])
                eng.wait_ge(sems[len(chunk_tot) - 1], chunk_tot[-1])
            return body

        blk.sync(mk_section("sync"))
        blk.tensor(mk_section("tensor"))
        blk.scalar(mk_section("scalar"))
        blk.vector(mk_section("vector"))


_NAMECTR = [0]


def _sb(nc, st, shape, dt):
    _NAMECTR[0] += 1
    return st.enter_context(nc.sbuf_tensor("sb%d" % _NAMECTR[0], shape, dt))


def _ps(nc, st):
    _NAMECTR[0] += 1
    return st.enter_context(nc.psum_tensor("ps%d" % _NAMECTR[0], [128, 512], F32))


def load_wbf(p, nc, st, src_ap, nch, width, scr):
    wb = _sb(nc, st, [128, nch, width], BF)
    for k in range(nch):
        for s0 in range(0, width, 512):
            w = min(512, width - s0)
            p.dma(scr[:, 0:w], src_ap[k * 128:(k + 1) * 128, s0:s0 + w])
            p.cp(wb[:, k, s0:s0 + w], scr[:, 0:w])
    return wb


def emit_ln(nc, p, st, xin, xnf, xnb):
    x_sb = _sb(nc, st, [128, 8, T], F32)
    xf_sb = _sb(nc, st, [128, 8, T], F32)
    xb_sb = _sb(nc, st, [128, 8, T], BF)
    onesf = _sb(nc, st, [128, 128], F32)
    eps = _sb(nc, st, [1, 1], F32)
    sq = _sb(nc, st, [128, 512], F32)
    scr2 = _sb(nc, st, [128, 512], F32)
    mean_sb = _sb(nc, st, [1, 512], F32)
    ex2_sb = _sb(nc, st, [1, 512], F32)
    m2_sb = _sb(nc, st, [1, 512], F32)
    var_sb = _sb(nc, st, [1, 512], F32)
    sd_sb = _sb(nc, st, [1, 512], F32)
    rstd_sb = _sb(nc, st, [1, 512], F32)
    pM = _ps(nc, st)
    pV = _ps(nc, st)
    pMB = _ps(nc, st)
    pRB = _ps(nc, st)
    p.memset(onesf[:, :], 1.0)
    p.memset(eps[:, :], 1e-5)
    for b in range(NB):
        for half in range(2):
            src = xin[b][half * 512:(half + 1) * 512, :]
            p.dma(x_sb[:, half * 4:(half + 1) * 4, :],
                  src.rearrange("(c p) t -> p c t", p=128))
        for qt in range(NQT):
            qs = slice(qt * 512, (qt + 1) * 512)
            for c in range(8):
                p.mm(pM[0:1, :], onesf[:, 0:1], x_sb[:, c, qs],
                     start=(c == 0), stop=(c == 7))
            for c in range(8):
                p.act(sq[:, :], x_sb[:, c, qs], AF.Square)
                p.mm(pV[0:1, :], onesf[:, 0:1], sq[:, :],
                     start=(c == 0), stop=(c == 7))
            p.act(mean_sb[:, :], pM[0:1, :], AF.Copy, scale=1.0 / 1024.0)
            p.act(ex2_sb[:, :], pV[0:1, :], AF.Copy, scale=1.0 / 1024.0)
            p.act(m2_sb[:, :], mean_sb[:, :], AF.Square)
            p.tt(var_sb[:, :], ex2_sb[:, :], m2_sb[:, :], OP.subtract)
            p.act(sd_sb[:, :], var_sb[:, :], AF.Sqrt, bias=eps[:, :])
            p.recip(rstd_sb[:, :], sd_sb[:, :])
            p.mm(pMB[:, :], onesf[0:1, 0:128], mean_sb[:, :])
            p.mm(pRB[:, :], onesf[0:1, 0:128], rstd_sb[:, :])
            for c in range(8):
                p.tt(scr2[:, :], x_sb[:, c, qs], pMB[:, :], OP.subtract)
                p.tt(xf_sb[:, c, qs], scr2[:, :], pRB[:, :], OP.mult)
                p.act(xb_sb[:, c, qs], xf_sb[:, c, qs], AF.Copy)
        for half in range(2):
            dst = xnf[b][half * 512:(half + 1) * 512, :]
            p.dma(dst.rearrange("(c p) t -> p c t", p=128),
                  xf_sb[:, half * 4:(half + 1) * 4, :])
            dstb = xnb[b][half * 512:(half + 1) * 512, :]
            p.dma(dstb.rearrange("(c p) t -> p c t", p=128),
                  xb_sb[:, half * 4:(half + 1) * 4, :])


def emit_qkv(nc, p, st, layer, xnb, wqk, wv, qk, vout, bqk=None, bv=None):
    scrw = _sb(nc, st, [128, 512], F32)
    wqk_bf = load_wbf(p, nc, st, wqk, 8, 2048, scrw)
    wv_bf = load_wbf(p, nc, st, wv, 8, 1024, scrw)
    xn_sb = _sb(nc, st, [128, 8, T], BF)
    row = _sb(nc, st, [128, 2048], BF)
    vrow = _sb(nc, st, [128, 1024], BF)
    pq = _ps(nc, st)
    pv = _ps(nc, st)
    if layer == 1:
        bqk_sb = _sb(nc, st, [128, 16], F32)
        p.dma(bqk_sb[:, :], bqk[:, :])
        bvf = _sb(nc, st, [1, D], F32)
        bvb = _sb(nc, st, [1, D], BF)
        p.dma(bvf[:, :], bv[:, :])
        p.cp(bvb[:, :], bvf[:, :])
        onesb = _sb(nc, st, [1, 128], BF)
        p.memset(onesb[:, :], 1.0)
    for b in range(NB):
        for half in range(2):
            src = xnb[b][half * 512:(half + 1) * 512, :]
            p.dma(xn_sb[:, half * 4:(half + 1) * 4, :],
                  src.rearrange("(c p) t -> p c t", p=128))
        for mc in range(16):
            for qt in range(NQT):
                qs = slice(qt * 512, (qt + 1) * 512)
                for k in range(8):
                    p.mm(pq[:, :], wqk_bf[:, k, mc * 128:(mc + 1) * 128],
                         xn_sb[:, k, qs], start=(k == 0), stop=(k == 7))
                if layer == 1:
                    p.ts1(row[:, qs], pq[:, :], bqk_sb[:, mc:mc + 1], OP.add)
                else:
                    p.act(row[:, qs], pq[:, :], AF.Copy)
            p.dma(qk[b, mc * 128:(mc + 1) * 128, :], row[:, :])
        for sc in range(16):
            for nch in range(2):
                ns = slice(nch * 512, (nch + 1) * 512)
                for k in range(8):
                    p.mm(pv[:, :], xn_sb[:, k, sc * 128:(sc + 1) * 128],
                         wv_bf[:, k, ns], start=(k == 0),
                         stop=(k == 7 and layer == 0))
                if layer == 1:
                    p.mm(pv[:, :], onesb[0:1, :], bvb[0:1, ns],
                         start=False, stop=True)
                p.act(vrow[:, ns], pv[:, :], AF.Copy)
            p.dma(vout[b, sc * 128:(sc + 1) * 128, :], vrow[:, :])


def emit_attn0(nc, p, st, qk, vin, wo, maskc, aout):
    scrw = _sb(nc, st, [128, 512], F32)
    wo_bf = load_wbf(p, nc, st, wo, 8, 1024, scrw)
    mk_sb = _sb(nc, st, [128, 4, 512], F32)
    p.dma(mk_sb[:, :, :], maskc.rearrange("m p q -> p m q"))
    v_sb = _sb(nc, st, [128, 16, 16, 65], BF)
    qh = _sb(nc, st, [64, T], BF)
    kh = _sb(nc, st, [64, T], BF)
    es_f = _sb(nc, st, [128, 16, 512], F32)
    es_b = _sb(nc, st, [128, 16, 512], BF)
    at_bf = _sb(nc, st, [128, 8, 512], BF)
    rden = _sb(nc, st, [1, 512], F32)
    avf = _sb(nc, st, [64, 512], F32)
    prj = _sb(nc, st, [128, 512], F32)
    onesf = _sb(nc, st, [1, 64], F32)
    p.memset(onesf[:, :], 1.0)
    pS = _ps(nc, st)
    pA = _ps(nc, st)
    pB = _ps(nc, st)
    pP = _ps(nc, st)
    for b in range(NB):
        p.memset(v_sb[:, :, :, 64:65], 1.0)
        for h in range(NH):
            for half in range(2):
                src = vin[b][half * 1024:(half + 1) * 1024,
                             h * 64:(h + 1) * 64]
                p.dma(v_sb[:, half * 8:(half + 1) * 8, h, 0:64],
                      src.rearrange("(kc p) d -> p kc d", p=128))
        for qt in range(NQT):
            qs = slice(qt * 512, (qt + 1) * 512)
            nkc = 4 * (qt + 1)
            for h in range(NH):
                p.dma(qh[:, :], qk[b, h * 64:(h + 1) * 64, :])
                p.dma(kh[:, :], qk[b, 1024 + h * 64:1024 + (h + 1) * 64, :])
                for kc in range(nkc):
                    p.mm(pS[:, :], kh[:, kc * 128:(kc + 1) * 128],
                         qh[:, qs])
                    if kc >= nkc - 4:
                        m = kc - 4 * qt
                        p.tt(es_f[:, kc, :], pS[:, :], mk_sb[:, m, :], OP.add)
                    else:
                        p.act(es_f[:, kc, :], pS[:, :], AF.Copy)
                p.act(es_b[:, 0:nkc, :], es_f[:, 0:nkc, :], AF.Exp,
                      scale=0.125)
                for kc in range(nkc):
                    p.mm(pA[0:65, :], v_sb[:, kc, h, :], es_b[:, kc, :],
                         start=(kc == 0), stop=(kc == nkc - 1))
                p.recip(rden[:, :], pA[64:65, :])
                p.mm(pB[0:64, :], onesf[0:1, 0:64], rden[:, :])
                p.act(avf[:, :], pA[0:64, :], AF.Copy)
                ho = (h % 2) * 64
                p.tt(at_bf[ho:ho + 64, h // 2, :], avf[:, :], pB[0:64, :],
                     OP.mult)
            for co in range(8):
                for k in range(8):
                    p.mm(pP[:, :], wo_bf[:, k, co * 128:(co + 1) * 128],
                         at_bf[:, k, :], start=(k == 0), stop=(k == 7))
                p.act(prj[:, :], pP[:, :], AF.Copy)
                p.dma(aout[b, co * 128:(co + 1) * 128, qs], prj[:, :])


def emit_attn1(nc, p, st, qk, vin, wo, bo, rvec, w1r, b1r, w2r, b2r,
               aout, biasT, ao):
    r_sb = _sb(nc, st, [1, 4095], F32)
    w1_sb = _sb(nc, st, [1, 32], F32)
    b1_sb = _sb(nc, st, [32, 1], F32)
    w2_sb = _sb(nc, st, [32, 16], F32)
    b2_sb = _sb(nc, st, [16, 1], F32)
    c8_16 = _sb(nc, st, [16, 1], F32)
    h_sb = _sb(nc, st, [32, 512], F32)
    tab = _sb(nc, st, [16, 4095], F32)
    p.dma(r_sb[:, :], rvec[:, :])
    p.dma(w1_sb[:, :], w1r[:, :])
    p.dma(b1_sb[:, :], b1r[:, :])
    p.dma(w2_sb[:, :], w2r[:, :])
    p.dma(b2_sb[:, :], b2r[:, :])
    pH = _ps(nc, st)
    pT = _ps(nc, st)
    p.memset(c8_16[:, :], 8.0)
    for u in range(8):
        w = 512 if u < 7 else 511
        s0 = u * 512
        p.mm(pH[0:32, 0:w], w1_sb[0:1, :], r_sb[0:1, s0:s0 + w])
        p.act(h_sb[0:32, 0:w], pH[0:32, 0:w], AF.Relu, bias=b1_sb[:, :])
        p.mm(pT[0:16, 0:w], w2_sb[0:32, :], h_sb[0:32, 0:w])
        p.ts2(tab[0:16, s0:s0 + w], pT[0:16, 0:w], b2_sb[:, :],
              c8_16[:, :], OP.add, OP.mult)
    for k in range(T):
        p.dma(biasT[:, k, :], tab[0:16, 2047 - k:4095 - k])
    q4 = _sb(nc, st, [64, 4, T], BF)
    k4 = _sb(nc, st, [64, 4, T], BF)
    v4 = _sb(nc, st, [128, 16, 4, 65], BF)
    bias_str = _sb(nc, st, [128, 16, 512], F32)
    es_f = _sb(nc, st, [128, 16, 512], F32)
    es_b = _sb(nc, st, [128, 16, 512], BF)
    rden = _sb(nc, st, [1, 512], F32)
    avf = _sb(nc, st, [64, 512], F32)
    avb = _sb(nc, st, [64, 512], BF)
    onesf = _sb(nc, st, [1, 64], F32)
    p.memset(onesf[:, :], 1.0)
    pS = _ps(nc, st)
    pA = _ps(nc, st)
    pB = _ps(nc, st)
    for h in range(NH):
        p.memset(v4[:, :, :, 64:65], 1.0)
        for b in range(NB):
            p.dma(q4[:, b, :], qk[b, h * 64:(h + 1) * 64, :])
            p.dma(k4[:, b, :],
                  qk[b, 1024 + h * 64:1024 + (h + 1) * 64, :])
            for half in range(2):
                src = vin[b][half * 1024:(half + 1) * 1024,
                             h * 64:(h + 1) * 64]
                p.dma(v4[:, half * 8:(half + 1) * 8, b, 0:64],
                      src.rearrange("(kc p) d -> p kc d", p=128))
        for qt in range(NQT):
            qs = slice(qt * 512, (qt + 1) * 512)
            for half in range(2):
                src = biasT[h][half * 1024:(half + 1) * 1024, qs]
                p.dma(bias_str[:, half * 8:(half + 1) * 8, :],
                      src.rearrange("(kc p) q -> p kc q", p=128))
            for b in range(NB):
                for kc in range(16):
                    p.mm(pS[:, :], k4[:, b, kc * 128:(kc + 1) * 128],
                         q4[:, b, qs])
                    p.tt(es_f[:, kc, :], pS[:, :], bias_str[:, kc, :], OP.add)
                p.act(es_b[:, :, :], es_f[:, :, :], AF.Exp, scale=0.125)
                for kc in range(16):
                    p.mm(pA[0:65, :], v4[:, kc, b, :], es_b[:, kc, :],
                         start=(kc == 0), stop=(kc == 15))
                p.recip(rden[:, :], pA[64:65, :])
                p.mm(pB[0:64, :], onesf[0:1, 0:64], rden[:, :])
                p.act(avf[:, :], pA[0:64, :], AF.Copy)
                p.tt(avb[:, :], avf[:, :], pB[0:64, :], OP.mult)
                p.dma(ao[b, h * 64:(h + 1) * 64, qs], avb[:, :])
    scrw = _sb(nc, st, [128, 512], F32)
    wo_bf = load_wbf(p, nc, st, wo, 8, 1024, scrw)
    bo_sb = _sb(nc, st, [128, 8], F32)
    p.dma(bo_sb[:, :], bo[:, :])
    as_sb = _sb(nc, st, [128, 8, 512], BF)
    prj = _sb(nc, st, [128, 512], F32)
    pP = _ps(nc, st)
    for b in range(NB):
        for qt in range(NQT):
            qs = slice(qt * 512, (qt + 1) * 512)
            for half in range(2):
                src = ao[b][half * 512:(half + 1) * 512, qs]
                p.dma(as_sb[:, half * 4:(half + 1) * 4, :],
                      src.rearrange("(c p) q -> p c q", p=128))
            for co in range(8):
                for k in range(8):
                    p.mm(pP[:, :], wo_bf[:, k, co * 128:(co + 1) * 128],
                         as_sb[:, k, :], start=(k == 0), stop=(k == 7))
                p.ts1(prj[:, :], pP[:, :], bo_sb[:, co:co + 1], OP.add)
                p.dma(aout[b, co * 128:(co + 1) * 128, qs], prj[:, :])


def emit_ffn0(nc, p, st, xin, attn, xnf, xnb, w1, w2, w3, w4,
              cb1, cb2, cb3, cb4, gw1, gw2, gw3, gb1, gb2, gb3, arin, x1):
    scrw = _sb(nc, st, [128, 512], F32)
    w1_bf = load_wbf(p, nc, st, w1, 8, 1024, scrw)
    w2_bf = load_wbf(p, nc, st, w2, 8, 512, scrw)
    w3_bf = load_wbf(p, nc, st, w3, 4, 256, scrw)
    gw1_bf = load_wbf(p, nc, st, gw1, 8, 512, scrw)
    gw2_bf = load_wbf(p, nc, st, gw2, 4, 256, scrw)
    w4_bf = _sb(nc, st, [128, 2, 1], BF)
    gw3_bf = _sb(nc, st, [128, 2, 1], BF)
    w4f = _sb(nc, st, [128, 2], F32)
    p.dma(w4f[:, :], w4.rearrange("(k p) o -> p (k o)", p=128), slow=True)
    p.cp(w4_bf[:, :, 0], w4f[:, :])
    p.dma(w4f[:, :], gw3.rearrange("(k p) o -> p (k o)", p=128), slow=True)
    p.cp(gw3_bf[:, :, 0], w4f[:, :])
    cb1_sb = _sb(nc, st, [128, 8], F32)
    cb2_sb = _sb(nc, st, [128, 4], F32)
    cb3_sb = _sb(nc, st, [128, 2], F32)
    cb4_sb = _sb(nc, st, [1, 1], F32)
    gb1_sb = _sb(nc, st, [128, 4], F32)
    gb2_sb = _sb(nc, st, [128, 2], F32)
    gb3_sb = _sb(nc, st, [1, 1], F32)
    ar_sb = _sb(nc, st, [1, 1], F32)
    for dst, src in [(cb1_sb, cb1), (cb2_sb, cb2), (cb3_sb, cb3),
                     (cb4_sb, cb4), (gb1_sb, gb1), (gb2_sb, gb2),
                     (gb3_sb, gb3), (ar_sb, arin)]:
        p.dma(dst[:, :], src[:, :])
    c01 = _sb(nc, st, [1, 1], F32)
    cm1 = _sb(nc, st, [1, 1], F32)
    ar2 = _sb(nc, st, [1, 1], F32)
    arc = _sb(nc, st, [1, 1], F32)
    p.memset(c01[:, :], 0.1)
    p.memset(cm1[:, :], -1.0)
    p.act(ar2[:, :], ar_sb[:, :], AF.Copy, scale=2.0)
    p.ts2(arc[:, :], ar_sb[:, :], cm1[:, :], c01[:, :], OP.mult, OP.add)
    onesf = _sb(nc, st, [1, 128], F32)
    p.memset(onesf[:, :], 1.0)
    xnb_sb = _sb(nc, st, [128, 8, T], BF)
    out_f = _sb(nc, st, [128, 8, T], F32)
    out_b = _sb(nc, st, [128, 8, T], BF)
    h1b = _sb(nc, st, [128, 8, 512], BF)
    h2b = _sb(nc, st, [128, 4, 512], BF)
    h3b = _sb(nc, st, [128, 2, 512], BF)
    tf_sb = _sb(nc, st, [1, 512], F32)
    cpl = _sb(nc, st, [1, 512], F32)
    gcf = _sb(nc, st, [128, 8, 1], F32)
    gcb = _sb(nc, st, [128, 8, 1], BF)
    g1f = _sb(nc, st, [128, 4, 1], F32)
    g1b = _sb(nc, st, [128, 4, 1], BF)
    g2f = _sb(nc, st, [128, 2, 1], F32)
    g2b = _sb(nc, st, [128, 2, 1], BF)
    gf_sb = _sb(nc, st, [1, 1], F32)
    gf7 = _sb(nc, st, [1, 1], F32)
    tA = _sb(nc, st, [128, 512], F32)
    tB = _sb(nc, st, [128, 512], F32)
    r1 = tA
    r2 = tB
    pC1 = _ps(nc, st)
    pC4 = _ps(nc, st)
    pBC = _ps(nc, st)
    pG = _ps(nc, st)
    for b in range(NB):
        for half in range(2):
            p.dma(xnb_sb[:, half * 4:(half + 1) * 4, :],
                  xnb[b][half * 512:(half + 1) * 512, :]
                  .rearrange("(c p) t -> p c t", p=128))
            p.dma(out_f[:, half * 4:(half + 1) * 4, :],
                  xnf[b][half * 512:(half + 1) * 512, :]
                  .rearrange("(c p) t -> p c t", p=128))
        for c in range(8):
            p.red(gcf[:, c, :], out_f[:, c, :])
        p.cp(gcb[:, :, :], gcf[:, :, :])
        for m in range(4):
            for k in range(8):
                p.mm(pG[:, 0:1], gw1_bf[:, k, m * 128:(m + 1) * 128],
                     gcb[:, k, :], start=(k == 0), stop=(k == 7))
            p.act(g1f[:, m, :], pG[:, 0:1], AF.Gelu,
                  bias=gb1_sb[:, m:m + 1], scale=1.0 / 2048.0)
        p.cp(g1b[:, :, :], g1f[:, :, :])
        for m in range(2):
            for k in range(4):
                p.mm(pG[:, 0:1], gw2_bf[:, k, m * 128:(m + 1) * 128],
                     g1b[:, k, :], start=(k == 0), stop=(k == 3))
            p.act(g2f[:, m, :], pG[:, 0:1], AF.Gelu,
                  bias=gb2_sb[:, m:m + 1])
        p.cp(g2b[:, :, :], g2f[:, :, :])
        for k in range(2):
            p.mm(pG[0:1, 0:1], gw3_bf[:, k, :], g2b[:, k, :],
                 start=(k == 0), stop=(k == 1))
        p.act(gf_sb[:, :], pG[0:1, 0:1], AF.Sigmoid, bias=gb3_sb[:, :])
        p.act(gf7[:, :], gf_sb[:, :], AF.Copy, scale=0.7)
        for cyc in range(3):
            rhs = xnb_sb if cyc == 0 else out_b
            for qt in range(NQT):
                qs = slice(qt * 512, (qt + 1) * 512)
                for m in range(8):
                    for k in range(8):
                        p.mm(pC1[:, :], w1_bf[:, k, m * 128:(m + 1) * 128],
                             rhs[:, k, qs], start=(k == 0), stop=(k == 7))
                    p.act(h1b[:, m, :], pC1[:, :], AF.Gelu,
                          bias=cb1_sb[:, m:m + 1])
                for m in range(4):
                    for k in range(8):
                        p.mm(pC1[:, :], w2_bf[:, k, m * 128:(m + 1) * 128],
                             h1b[:, k, :], start=(k == 0), stop=(k == 7))
                    p.act(h2b[:, m, :], pC1[:, :], AF.Gelu,
                          bias=cb2_sb[:, m:m + 1])
                for m in range(2):
                    for k in range(4):
                        p.mm(pC1[:, :], w3_bf[:, k, m * 128:(m + 1) * 128],
                             h2b[:, k, :], start=(k == 0), stop=(k == 3))
                    p.act(h3b[:, m, :], pC1[:, :], AF.Gelu,
                          bias=cb3_sb[:, m:m + 1])
                for k in range(2):
                    p.mm(pC4[0:1, :], w4_bf[:, k, :], h3b[:, k, :],
                         start=(k == 0), stop=(k == 1))
                p.act(tf_sb[:, :], pC4[0:1, :], AF.Sigmoid, bias=cb4_sb[:, :])
                p.act(tf_sb[:, :], tf_sb[:, :], AF.Copy, scale=0.3)
                p.ts1(tf_sb[:, :], tf_sb[:, :], gf7[:, :], OP.add)
                p.ts2(cpl[:, :], tf_sb[:, :], ar2[:, :], arc[:, :],
                      OP.mult, OP.add)
                p.mm(pBC[:, :], onesf[0:1, :], cpl[:, :])
                for (a, bb, sa) in [(0, 4, 1), (1, 5, 1), (2, 6, -1),
                                    (3, 7, -1)]:
                    p.tt(tA[:, :], out_f[:, bb, qs], pBC[:, :], OP.mult)
                    p.tt(tB[:, :], out_f[:, a, qs], pBC[:, :], OP.mult)
                    p.tt(out_f[:, a, qs], out_f[:, a, qs], tA[:, :],
                         OP.add if sa == 1 else OP.subtract)
                    p.tt(out_f[:, bb, qs], out_f[:, bb, qs], tB[:, :],
                         OP.subtract if sa == 1 else OP.add)
                if cyc < 2:
                    for c in range(8):
                        p.act(out_b[:, c, qs], out_f[:, c, qs], AF.Copy)
        for c in range(8):
            for qt in range(NQT):
                qs = slice(qt * 512, (qt + 1) * 512)
                p.dma(r1[:, :], xin[b, c * 128:(c + 1) * 128, qs])
                p.dma(r2[:, :], attn[b, c * 128:(c + 1) * 128, qs])
                p.tt(r1[:, :], r1[:, :], out_f[:, c, qs], OP.add)
                p.tt(r1[:, :], r1[:, :], r2[:, :], OP.add)
                p.dma(x1[b, c * 128:(c + 1) * 128, qs], r1[:, :])


def emit_ffn1(nc, p, st, x1, attn, xnb, wg, wu, wd, yout, hh):
    st1 = ExitStack()
    wint = _sb(nc, st1, [128, 3072], I32)
    wf = _sb(nc, st1, [128, 3072], F32)
    wg_bf = _sb(nc, st1, [128, 8, 3072], BF)
    wu_bf = _sb(nc, st1, [128, 8, 3072], BF)
    xnb_sb = _sb(nc, st1, [128, 8, T], BF)
    t1 = _sb(nc, st1, [128, 512], F32)
    hscr = _sb(nc, st1, [128, 512], BF)
    for k in range(8):
        p.dma(wint[:, :], wg[k * 128:(k + 1) * 128, :])
        p.cp(wf[:, :], wint[:, :])
        p.cp(wg_bf[:, k, :], wf[:, :])
        p.dma(wint[:, :], wu[k * 128:(k + 1) * 128, :])
        p.cp(wf[:, :], wint[:, :])
        p.cp(wu_bf[:, k, :], wf[:, :])
    pG1 = _ps(nc, st1)
    pU1 = _ps(nc, st1)
    for b in range(NB):
        for half in range(2):
            p.dma(xnb_sb[:, half * 4:(half + 1) * 4, :],
                  xnb[b][half * 512:(half + 1) * 512, :]
                  .rearrange("(c p) t -> p c t", p=128))
        for qt in range(NQT):
            qs = slice(qt * 512, (qt + 1) * 512)
            for mc in range(24):
                for k in range(8):
                    p.mm(pG1[:, :], wg_bf[:, k, mc * 128:(mc + 1) * 128],
                         xnb_sb[:, k, qs], start=(k == 0), stop=(k == 7))
                for k in range(8):
                    p.mm(pU1[:, :], wu_bf[:, k, mc * 128:(mc + 1) * 128],
                         xnb_sb[:, k, qs], start=(k == 0), stop=(k == 7))
                p.act(t1[:, :], pG1[:, :], AF.Silu, scale=1.0 / 127.0)
                p.tt(hscr[:, :], t1[:, :], pU1[:, :], OP.mult)
                p.dma(hh[b, mc * 128:(mc + 1) * 128, qs], hscr[:, :])
    st1.close()
    wint2 = _sb(nc, st, [128, 1024], I32)
    wf2 = _sb(nc, st, [128, 1024], F32)
    wd_bf = _sb(nc, st, [128, 24, 1024], BF)
    hs = _sb(nc, st, [128, 24, 512], BF)
    yscr = _sb(nc, st, [128, 512], F32)
    r1 = _sb(nc, st, [128, 512], F32)
    r2 = _sb(nc, st, [128, 512], F32)
    for k in range(24):
        p.dma(wint2[:, :], wd[k * 128:(k + 1) * 128, :])
        p.cp(wf2[:, :], wint2[:, :])
        p.cp(wd_bf[:, k, :], wf2[:, :])
    pD = _ps(nc, st)
    for b in range(NB):
        for qt in range(NQT):
            qs = slice(qt * 512, (qt + 1) * 512)
            for third in range(3):
                src = hh[b][third * 1024:(third + 1) * 1024, qs]
                p.dma(hs[:, third * 8:(third + 1) * 8, :],
                      src.rearrange("(m p) q -> p m q", p=128))
            for co in range(8):
                for k in range(24):
                    p.mm(pD[:, :], wd_bf[:, k, co * 128:(co + 1) * 128],
                         hs[:, k, :], start=(k == 0), stop=(k == 23))
                p.act(yscr[:, :], pD[:, :], AF.Copy, scale=1.0 / 16129.0)
                p.dma(r1[:, :], x1[b, co * 128:(co + 1) * 128, qs])
                p.dma(r2[:, :], attn[b, co * 128:(co + 1) * 128, qs])
                p.tt(yscr[:, :], yscr[:, :], r1[:, :], OP.add)
                p.tt(yscr[:, :], yscr[:, :], r2[:, :], OP.add)
                p.dma(yout[b, co * 128:(co + 1) * 128, qs], yscr[:, :])


def build_all():
    nc = bass.Bass("TRN2", target_bir_lowering=False)
    st = ExitStack()

    def ein(name, shape, dt):
        return nc.dram_tensor(name, shape, dt, kind="ExternalInput").ap()

    def internal(name, shape, dt):
        return nc.dram_tensor(name, shape, dt, kind="Internal").ap()

    x = ein("x", [NB, D, T], F32)
    wqk0 = ein("wqk0", [D, 2048], F32)
    wv0 = ein("wv0", [D, D], F32)
    wo0 = ein("wo0", [D, D], F32)
    maskc = ein("maskc", [4, 128, 512], F32)
    w1 = ein("w1", [1024, 1024], F32)
    w2 = ein("w2", [1024, 512], F32)
    w3 = ein("w3", [512, 256], F32)
    w4 = ein("w4", [256, 1], F32)
    cb1 = ein("cb1", [128, 8], F32)
    cb2 = ein("cb2", [128, 4], F32)
    cb3 = ein("cb3", [128, 2], F32)
    cb4 = ein("cb4", [1, 1], F32)
    gw1 = ein("gw1", [1024, 512], F32)
    gw2 = ein("gw2", [512, 256], F32)
    gw3 = ein("gw3", [256, 1], F32)
    gb1 = ein("gb1", [128, 4], F32)
    gb2 = ein("gb2", [128, 2], F32)
    gb3 = ein("gb3", [1, 1], F32)
    arin = ein("ar", [1, 1], F32)
    wqk1 = ein("wqk1", [D, 2048], F32)
    wv1 = ein("wv1", [D, D], F32)
    bqk1 = ein("bqk1", [128, 16], F32)
    bv1 = ein("bv1", [1, D], F32)
    wo1 = ein("wo1", [D, D], F32)
    bo1 = ein("bo1", [128, 8], F32)
    rvec = ein("rvec", [1, 4095], F32)
    w1r = ein("w1r", [1, 32], F32)
    b1r = ein("b1r", [32, 1], F32)
    w2r = ein("w2r", [32, 16], F32)
    b2r = ein("b2r", [16, 1], F32)
    wg = ein("wg", [1024, 3072], I32)
    wu = ein("wu", [1024, 3072], I32)
    wd = ein("wd", [3072, 1024], I32)
    y = nc.dram_tensor("y", [NB, D, T], F32, kind="ExternalOutput").ap()

    xnf0 = internal("xnf0", [NB, D, T], F32)
    xnb0 = internal("xnb0", [NB, D, T], BF)
    qk0 = internal("qk0", [NB, 2048, T], BF)
    v0 = internal("v0", [NB, T, D], BF)
    at0 = internal("at0", [NB, D, T], F32)
    x1d = internal("x1d", [NB, D, T], F32)
    xnf1 = internal("xnf1", [NB, D, T], F32)
    xnb1 = internal("xnb1", [NB, D, T], BF)
    qk1 = internal("qk1", [NB, 2048, T], BF)
    v1 = internal("v1", [NB, T, D], BF)
    biasT = internal("biasT", [NH, T, T], F32)
    ao = internal("ao", [NB, D, T], BF)
    at1 = internal("at1", [NB, D, T], F32)
    hh = internal("hh", [NB, 3072, T], BF)

    p = Prog(nc)
    stages = [
        lambda s: emit_ln(nc, p, s, x, xnf0, xnb0),
        lambda s: emit_qkv(nc, p, s, 0, xnb0, wqk0, wv0, qk0, v0),
        lambda s: emit_attn0(nc, p, s, qk0, v0, wo0, maskc, at0),
        lambda s: emit_ffn0(nc, p, s, x, at0, xnf0, xnb0, w1, w2, w3, w4,
                            cb1, cb2, cb3, cb4, gw1, gw2, gw3, gb1, gb2,
                            gb3, arin, x1d),
        lambda s: emit_ln(nc, p, s, x1d, xnf1, xnb1),
        lambda s: emit_qkv(nc, p, s, 1, xnb1, wqk1, wv1, qk1, v1,
                           bqk=bqk1, bv=bv1),
        lambda s: emit_attn1(nc, p, s, qk1, v1, wo1, bo1, rvec, w1r, b1r,
                             w2r, b2r, at1, biasT, ao),
        lambda s: emit_ffn1(nc, p, s, x1d, at1, xnb1, wg, wu, wd, y, hh),
    ]
    for stg in stages:
        sub = ExitStack()
        stg(sub)
        sub.close()
    p.finalize(st)
    st.close()
    return nc


_PROGS = {}


def _bias_cols(b):
    n = b.shape[0]
    return np.ascontiguousarray(b.reshape(n // 128, 128).T).astype(np.float32)


def kernel(**ins):
    f32 = np.float32
    x = np.asarray(ins["L0_attn_x"], f32)
    cores = list(range(8))
    xT = [np.ascontiguousarray(x[4 * i:4 * i + 4].transpose(0, 2, 1))
          for i in cores]

    Wqkv0 = np.asarray(ins["L0_attn_W_qkv"], f32)
    wqk0 = np.ascontiguousarray(Wqkv0[:2048].T)
    wv0 = np.ascontiguousarray(Wqkv0[2048:].T)
    wo0 = np.ascontiguousarray(np.asarray(ins["L0_attn_W_out"], f32).T)
    ii = np.arange(128)[:, None]
    jj = np.arange(512)[None, :]
    maskc = np.stack([np.where(jj >= ii + 128 * m, 0.0, -1e9)
                      for m in range(4)]).astype(f32)
    maskc = np.ascontiguousarray(maskc)

    wqkv1 = np.asarray(ins["L1_attn_w_qkv"], f32)
    wqk1 = np.ascontiguousarray(wqkv1[:, :2048])
    wv1 = np.ascontiguousarray(wqkv1[:, 2048:])
    bqkv1 = np.asarray(ins["L1_attn_b_qkv"], f32)
    bqk1 = _bias_cols(bqkv1[:2048])
    bv1 = np.ascontiguousarray(bqkv1[2048:].reshape(1, 1024))
    wo1 = np.ascontiguousarray(np.asarray(ins["L1_attn_w_out"], f32))
    bo1 = _bias_cols(np.asarray(ins["L1_attn_b_out"], f32))
    rel = np.arange(4095, dtype=np.float64) - 2047.0
    rvec = (np.sign(rel) * np.log1p(np.abs(rel))).astype(f32).reshape(1, 4095)
    w1r = np.ascontiguousarray(np.asarray(ins["L1_attn_w1_rpe"], f32))
    b1r = np.ascontiguousarray(
        np.asarray(ins["L1_attn_b1_rpe"], f32).reshape(32, 1))
    w2r = np.ascontiguousarray(np.asarray(ins["L1_attn_w2_rpe"], f32))
    b2r = np.ascontiguousarray(
        np.asarray(ins["L1_attn_b2_rpe"], f32).reshape(16, 1))

    wgT = np.ascontiguousarray(np.asarray(ins["L1_ffn_w_gate"]).T).astype(np.int32)
    wuT = np.ascontiguousarray(np.asarray(ins["L1_ffn_w_up"]).T).astype(np.int32)
    wdT = np.ascontiguousarray(np.asarray(ins["L1_ffn_w_down"]).T).astype(np.int32)

    shared = {
        "wqk0": wqk0, "wv0": wv0, "wo0": wo0, "maskc": maskc,
        "w1": np.ascontiguousarray(np.asarray(ins["L0_ffn_cn_w1"], f32)),
        "w2": np.ascontiguousarray(np.asarray(ins["L0_ffn_cn_w2"], f32)),
        "w3": np.ascontiguousarray(np.asarray(ins["L0_ffn_cn_w3"], f32)),
        "w4": np.ascontiguousarray(np.asarray(ins["L0_ffn_cn_w4"], f32)),
        "cb1": _bias_cols(np.asarray(ins["L0_ffn_cn_b1"], f32)),
        "cb2": _bias_cols(np.asarray(ins["L0_ffn_cn_b2"], f32)),
        "cb3": _bias_cols(np.asarray(ins["L0_ffn_cn_b3"], f32)),
        "cb4": np.asarray(ins["L0_ffn_cn_b4"], f32).reshape(1, 1),
        "gw1": np.ascontiguousarray(np.asarray(ins["L0_ffn_gc_w1"], f32)),
        "gw2": np.ascontiguousarray(np.asarray(ins["L0_ffn_gc_w2"], f32)),
        "gw3": np.ascontiguousarray(np.asarray(ins["L0_ffn_gc_w3"], f32)),
        "gb1": _bias_cols(np.asarray(ins["L0_ffn_gc_b1"], f32)),
        "gb2": _bias_cols(np.asarray(ins["L0_ffn_gc_b2"], f32)),
        "gb3": np.asarray(ins["L0_ffn_gc_b3"], f32).reshape(1, 1),
        "ar": np.asarray(ins["L0_ffn_adaptive_range"], f32).reshape(1, 1),
        "wqk1": wqk1, "wv1": wv1, "bqk1": bqk1, "bv1": bv1,
        "wo1": wo1, "bo1": bo1, "rvec": rvec, "w1r": w1r, "b1r": b1r,
        "w2r": w2r, "b2r": b2r, "wg": wgT, "wu": wuT, "wd": wdT,
    }
    if "all" not in _PROGS:
        _PROGS["all"] = build_all()
    nc = _PROGS["all"]
    in_maps = [dict(shared, x=xT[i]) for i in cores]
    res = run_bass_kernel_spmd(nc, in_maps, core_ids=cores)
    y = [np.asarray(res.results[i]["y"], f32).transpose(0, 2, 1)
         for i in cores]
    return np.ascontiguousarray(np.concatenate(y, axis=0))


# version 03
# speedup vs baseline: 1.2140x; 1.2140x over previous
import numpy as np
from contextlib import ExitStack
import ml_dtypes
import concourse.bass as bass
import concourse.mybir as mybir
from concourse.bass_utils import run_bass_kernel_spmd

F32 = mybir.dt.float32
BF = mybir.dt.bfloat16
I32 = mybir.dt.int32
AF = mybir.ActivationFunctionType
OP = mybir.AluOpType
AX = mybir.AxisListType

NB = 4
T = 2048
D = 1024
NH = 16
HD = 64
NQT = 4
BFNP = ml_dtypes.bfloat16


class Prog:
    def __init__(self, nc):
        self.nc = nc
        self.ops = []

    def mm(self, out, l, r, start=True, stop=True, chain=False):
        nc = self.nc
        self.ops.append(("tensor",
                         lambda o=out, a=l, b=r, s=start, e=stop:
                         nc.tensor.matmul(o, a, b, start=s, stop=e), 1,
                         chain and not start))

    def act(self, out, in_, f, bias=None, scale=None):
        nc = self.nc
        kw = {}
        if bias is not None:
            kw["bias"] = bias
        if scale is not None:
            kw["scale"] = scale
        self.ops.append(("scalar",
                         lambda o=out, i=in_, fn=f, k=dict(kw):
                         nc.scalar.activation(o, i, fn, **k), 1))

    def tt(self, out, a, b, op):
        nc = self.nc
        self.ops.append(("vector",
                         lambda o=out, x=a, y=b, p=op:
                         nc.vector.tensor_tensor(o, x, y, p), 1))

    def ts1(self, out, a, s1, op):
        nc = self.nc
        self.ops.append(("vector",
                         lambda o=out, x=a, s=s1, p=op:
                         nc.vector.tensor_scalar(o, x, s, None, p), 1))

    def ts2(self, out, a, s1, s2, op0, op1):
        nc = self.nc
        self.ops.append(("vector",
                         lambda o=out, x=a, u=s1, v=s2, p=op0, q=op1:
                         nc.vector.tensor_scalar(o, x, u, v, p, q), 1))

    def cp(self, out, in_):
        nc = self.nc
        self.ops.append(("vector",
                         lambda o=out, i=in_: nc.vector.tensor_copy(o, i), 1))

    def recip(self, out, in_):
        nc = self.nc
        self.ops.append(("vector",
                         lambda o=out, i=in_: nc.vector.reciprocal(o, i), 1))

    def memset(self, ap, v):
        nc = self.nc
        self.ops.append(("vector",
                         lambda a=ap, x=v: nc.vector.memset(a, x), 1))

    def red(self, out, in_):
        nc = self.nc
        self.ops.append(("vector",
                         lambda o=out, i=in_:
                         nc.vector.reduce_sum(o, i, AX.X), 1))

    def dma(self, dst, src, slow=False):
        nc = self.nc
        kw = {}
        if slow:
            kw["allow_slow_non_contiguous"] = True
        self.ops.append(("sync",
                         lambda d=dst, s=src, k=dict(kw):
                         nc.sync.dma_start(d, s, **k), 16))

    def finalize(self, st):
        nc = self.nc
        CH = 20000
        meta = []
        chunk_tot = []
        cur = 0
        cum = 0
        for op in self.ops:
            eng, fn, inc = op[0], op[1], op[2]
            nowait = op[3] if len(op) > 3 else False
            if cum + inc > CH:
                chunk_tot.append(cum)
                cur += 1
                cum = 0
            meta.append((eng, fn, inc, cur, cum, nowait))
            cum += inc
        chunk_tot.append(cum)
        sems = [st.enter_context(nc.semaphore("sem%d" % i))
                for i in range(len(chunk_tot))]
        blk = st.enter_context(nc.Block())

        def mk_section(ename):
            def body(eng):
                seen = set()
                for (e, fn, inc, ch, wv, nowait) in meta:
                    if e != ename:
                        continue
                    if not nowait:
                        if ch not in seen:
                            seen.add(ch)
                            if ch > 0:
                                eng.wait_ge(sems[ch - 1], chunk_tot[ch - 1])
                        eng.wait_ge(sems[ch], wv)
                    fn().then_inc(sems[ch], inc)
                if len(chunk_tot) > 1:
                    eng.wait_ge(sems[len(chunk_tot) - 2], chunk_tot[-2])
                eng.wait_ge(sems[len(chunk_tot) - 1], chunk_tot[-1])
            return body

        blk.sync(mk_section("sync"))
        blk.tensor(mk_section("tensor"))
        blk.scalar(mk_section("scalar"))
        blk.vector(mk_section("vector"))


_NAMECTR = [0]


def _sb(nc, st, shape, dt):
    _NAMECTR[0] += 1
    return st.enter_context(nc.sbuf_tensor("sb%d" % _NAMECTR[0], shape, dt))


def _ps(nc, st):
    _NAMECTR[0] += 1
    return st.enter_context(nc.psum_tensor("ps%d" % _NAMECTR[0], [128, 512], F32))


def load_wbf(p, nc, st, src_ap, nch, width, scr):
    wb = _sb(nc, st, [128, nch, width], BF)
    for k in range(nch):
        for s0 in range(0, width, 512):
            w = min(512, width - s0)
            p.dma(scr[:, 0:w], src_ap[k * 128:(k + 1) * 128, s0:s0 + w])
            p.cp(wb[:, k, s0:s0 + w], scr[:, 0:w])
    return wb


def emit_ln(nc, p, st, xin, xnf, xnb):
    x_sb = _sb(nc, st, [128, 8, T], F32)
    xf_sb = _sb(nc, st, [128, 8, T], F32)
    xb_sb = _sb(nc, st, [128, 8, T], BF)
    onesf = _sb(nc, st, [128, 128], F32)
    eps = _sb(nc, st, [1, 1], F32)
    sq = _sb(nc, st, [128, 512], F32)
    scr2 = _sb(nc, st, [128, 512], F32)
    mean_sb = _sb(nc, st, [1, 512], F32)
    ex2_sb = _sb(nc, st, [1, 512], F32)
    m2_sb = _sb(nc, st, [1, 512], F32)
    var_sb = _sb(nc, st, [1, 512], F32)
    sd_sb = _sb(nc, st, [1, 512], F32)
    rstd_sb = _sb(nc, st, [1, 512], F32)
    pM = _ps(nc, st)
    pV = _ps(nc, st)
    pMB = _ps(nc, st)
    pRB = _ps(nc, st)
    p.memset(onesf[:, :], 1.0)
    p.memset(eps[:, :], 1e-5)
    for b in range(NB):
        for half in range(2):
            src = xin[b][half * 512:(half + 1) * 512, :]
            p.dma(x_sb[:, half * 4:(half + 1) * 4, :],
                  src.rearrange("(c p) t -> p c t", p=128))
        for qt in range(NQT):
            qs = slice(qt * 512, (qt + 1) * 512)
            for c in range(8):
                p.mm(pM[0:1, :], onesf[:, 0:1], x_sb[:, c, qs],
                     start=(c == 0), stop=(c == 7), chain=True)
            for c in range(8):
                p.act(sq[:, :], x_sb[:, c, qs], AF.Square)
                p.mm(pV[0:1, :], onesf[:, 0:1], sq[:, :],
                     start=(c == 0), stop=(c == 7))
            p.act(mean_sb[:, :], pM[0:1, :], AF.Copy, scale=1.0 / 1024.0)
            p.act(ex2_sb[:, :], pV[0:1, :], AF.Copy, scale=1.0 / 1024.0)
            p.act(m2_sb[:, :], mean_sb[:, :], AF.Square)
            p.tt(var_sb[:, :], ex2_sb[:, :], m2_sb[:, :], OP.subtract)
            p.act(sd_sb[:, :], var_sb[:, :], AF.Sqrt, bias=eps[:, :])
            p.recip(rstd_sb[:, :], sd_sb[:, :])
            p.mm(pMB[:, :], onesf[0:1, 0:128], mean_sb[:, :])
            p.mm(pRB[:, :], onesf[0:1, 0:128], rstd_sb[:, :])
            for c in range(8):
                p.tt(scr2[:, :], x_sb[:, c, qs], pMB[:, :], OP.subtract)
                p.tt(xf_sb[:, c, qs], scr2[:, :], pRB[:, :], OP.mult)
                p.act(xb_sb[:, c, qs], xf_sb[:, c, qs], AF.Copy)
        for half in range(2):
            dst = xnf[b][half * 512:(half + 1) * 512, :]
            p.dma(dst.rearrange("(c p) t -> p c t", p=128),
                  xf_sb[:, half * 4:(half + 1) * 4, :])
            dstb = xnb[b][half * 512:(half + 1) * 512, :]
            p.dma(dstb.rearrange("(c p) t -> p c t", p=128),
                  xb_sb[:, half * 4:(half + 1) * 4, :])


def emit_qkv(nc, p, st, layer, xnb, wqk, wv, qk, vout, bqk=None, bv=None):
    scrw = _sb(nc, st, [128, 512], F32)
    wqk_bf = load_wbf(p, nc, st, wqk, 8, 2048, scrw)
    wv_bf = load_wbf(p, nc, st, wv, 8, 1024, scrw)
    xn_sb = _sb(nc, st, [128, 8, T], BF)
    row = _sb(nc, st, [128, 2048], BF)
    vrow = _sb(nc, st, [128, 1024], BF)
    pq = _ps(nc, st)
    pv = _ps(nc, st)
    if layer == 1:
        bqk_sb = _sb(nc, st, [128, 16], F32)
        p.dma(bqk_sb[:, :], bqk[:, :])
        bvf = _sb(nc, st, [1, D], F32)
        bvb = _sb(nc, st, [1, D], BF)
        p.dma(bvf[:, :], bv[:, :])
        p.cp(bvb[:, :], bvf[:, :])
        onesb = _sb(nc, st, [1, 128], BF)
        p.memset(onesb[:, :], 1.0)
    for b in range(NB):
        for half in range(2):
            src = xnb[b][half * 512:(half + 1) * 512, :]
            p.dma(xn_sb[:, half * 4:(half + 1) * 4, :],
                  src.rearrange("(c p) t -> p c t", p=128))
        for mc in range(16):
            for qt in range(NQT):
                qs = slice(qt * 512, (qt + 1) * 512)
                for k in range(8):
                    p.mm(pq[:, :], wqk_bf[:, k, mc * 128:(mc + 1) * 128],
                         xn_sb[:, k, qs], start=(k == 0), stop=(k == 7),
                         chain=True)
                if layer == 1:
                    p.ts1(row[:, qs], pq[:, :], bqk_sb[:, mc:mc + 1], OP.add)
                else:
                    p.act(row[:, qs], pq[:, :], AF.Copy)
            p.dma(qk[b, mc * 128:(mc + 1) * 128, :], row[:, :])
        for sc in range(16):
            for nch in range(2):
                ns = slice(nch * 512, (nch + 1) * 512)
                for k in range(8):
                    p.mm(pv[:, :], xn_sb[:, k, sc * 128:(sc + 1) * 128],
                         wv_bf[:, k, ns], start=(k == 0),
                         stop=(k == 7 and layer == 0), chain=True)
                if layer == 1:
                    p.mm(pv[:, :], onesb[0:1, :], bvb[0:1, ns],
                         start=False, stop=True, chain=True)
                p.act(vrow[:, ns], pv[:, :], AF.Copy)
            p.dma(vout[b, sc * 128:(sc + 1) * 128, :], vrow[:, :])


def emit_attn0(nc, p, st, qk, vin, wo, maskc, aout):
    scrw = _sb(nc, st, [128, 512], F32)
    wo_bf = load_wbf(p, nc, st, wo, 8, 1024, scrw)
    mk_sb = _sb(nc, st, [128, 4, 512], F32)
    p.dma(mk_sb[:, :, :], maskc.rearrange("m p q -> p m q"))
    v_sb = _sb(nc, st, [128, 16, 16, 65], BF)
    qh = _sb(nc, st, [64, T], BF)
    kh = _sb(nc, st, [64, T], BF)
    es_f = _sb(nc, st, [128, 16, 512], F32)
    es_b = _sb(nc, st, [128, 16, 512], BF)
    at_bf = _sb(nc, st, [128, 8, 512], BF)
    rden = _sb(nc, st, [1, 512], F32)
    avf = _sb(nc, st, [64, 512], F32)
    prj = _sb(nc, st, [128, 512], F32)
    onesf = _sb(nc, st, [1, 64], F32)
    p.memset(onesf[:, :], 1.0)
    pS = _ps(nc, st)
    pA = _ps(nc, st)
    pB = _ps(nc, st)
    pP = _ps(nc, st)
    for b in range(NB):
        p.memset(v_sb[:, :, :, 64:65], 1.0)
        for h in range(NH):
            for half in range(2):
                src = vin[b][half * 1024:(half + 1) * 1024,
                             h * 64:(h + 1) * 64]
                p.dma(v_sb[:, half * 8:(half + 1) * 8, h, 0:64],
                      src.rearrange("(kc p) d -> p kc d", p=128))
        for qt in range(NQT):
            qs = slice(qt * 512, (qt + 1) * 512)
            nkc = 4 * (qt + 1)
            for h in range(NH):
                p.dma(qh[:, :], qk[b, h * 64:(h + 1) * 64, :])
                p.dma(kh[:, :], qk[b, 1024 + h * 64:1024 + (h + 1) * 64, :])
                for kc in range(nkc):
                    p.mm(pS[:, :], kh[:, kc * 128:(kc + 1) * 128],
                         qh[:, qs])
                    if kc >= nkc - 4:
                        m = kc - 4 * qt
                        p.tt(es_f[:, kc, :], pS[:, :], mk_sb[:, m, :], OP.add)
                    else:
                        p.act(es_f[:, kc, :], pS[:, :], AF.Copy)
                p.act(es_b[:, 0:nkc, :], es_f[:, 0:nkc, :], AF.Exp,
                      scale=0.125)
                for kc in range(nkc):
                    p.mm(pA[0:65, :], v_sb[:, kc, h, :], es_b[:, kc, :],
                         start=(kc == 0), stop=(kc == nkc - 1), chain=True)
                p.recip(rden[:, :], pA[64:65, :])
                p.mm(pB[0:64, :], onesf[0:1, 0:64], rden[:, :])
                p.act(avf[:, :], pA[0:64, :], AF.Copy)
                ho = (h % 2) * 64
                p.tt(at_bf[ho:ho + 64, h // 2, :], avf[:, :], pB[0:64, :],
                     OP.mult)
            for co in range(8):
                for k in range(8):
                    p.mm(pP[:, :], wo_bf[:, k, co * 128:(co + 1) * 128],
                         at_bf[:, k, :], start=(k == 0), stop=(k == 7),
                         chain=True)
                p.act(prj[:, :], pP[:, :], AF.Copy)
                p.dma(aout[b, co * 128:(co + 1) * 128, qs], prj[:, :])


def emit_attn1(nc, p, st, qk, vin, wo, bo, rvec, w1r, b1r, w2r, b2r,
               aout, biasT, ao):
    r_sb = _sb(nc, st, [1, 4095], F32)
    w1_sb = _sb(nc, st, [1, 32], F32)
    b1_sb = _sb(nc, st, [32, 1], F32)
    w2_sb = _sb(nc, st, [32, 16], F32)
    b2_sb = _sb(nc, st, [16, 1], F32)
    c8_16 = _sb(nc, st, [16, 1], F32)
    h_sb = _sb(nc, st, [32, 512], F32)
    tab = _sb(nc, st, [16, 4095], F32)
    p.dma(r_sb[:, :], rvec[:, :])
    p.dma(w1_sb[:, :], w1r[:, :])
    p.dma(b1_sb[:, :], b1r[:, :])
    p.dma(w2_sb[:, :], w2r[:, :])
    p.dma(b2_sb[:, :], b2r[:, :])
    pH = _ps(nc, st)
    pT = _ps(nc, st)
    p.memset(c8_16[:, :], 8.0)
    for u in range(8):
        w = 512 if u < 7 else 511
        s0 = u * 512
        p.mm(pH[0:32, 0:w], w1_sb[0:1, :], r_sb[0:1, s0:s0 + w])
        p.act(h_sb[0:32, 0:w], pH[0:32, 0:w], AF.Relu, bias=b1_sb[:, :])
        p.mm(pT[0:16, 0:w], w2_sb[0:32, :], h_sb[0:32, 0:w])
        p.ts2(tab[0:16, s0:s0 + w], pT[0:16, 0:w], b2_sb[:, :],
              c8_16[:, :], OP.add, OP.mult)
    for k in range(T):
        p.dma(biasT[:, k, :], tab[0:16, 2047 - k:4095 - k])
    q4 = _sb(nc, st, [64, 4, T], BF)
    k4 = _sb(nc, st, [64, 4, T], BF)
    v4 = _sb(nc, st, [128, 16, 4, 65], BF)
    bias_str = _sb(nc, st, [128, 16, 512], F32)
    es_f = _sb(nc, st, [128, 16, 512], F32)
    es_b = _sb(nc, st, [128, 16, 512], BF)
    rden = _sb(nc, st, [1, 512], F32)
    avf = _sb(nc, st, [64, 512], F32)
    avb = _sb(nc, st, [64, 512], BF)
    onesf = _sb(nc, st, [1, 64], F32)
    p.memset(onesf[:, :], 1.0)
    pS = _ps(nc, st)
    pA = _ps(nc, st)
    pB = _ps(nc, st)
    for h in range(NH):
        p.memset(v4[:, :, :, 64:65], 1.0)
        for b in range(NB):
            p.dma(q4[:, b, :], qk[b, h * 64:(h + 1) * 64, :])
            p.dma(k4[:, b, :],
                  qk[b, 1024 + h * 64:1024 + (h + 1) * 64, :])
            for half in range(2):
                src = vin[b][half * 1024:(half + 1) * 1024,
                             h * 64:(h + 1) * 64]
                p.dma(v4[:, half * 8:(half + 1) * 8, b, 0:64],
                      src.rearrange("(kc p) d -> p kc d", p=128))
        for qt in range(NQT):
            qs = slice(qt * 512, (qt + 1) * 512)
            for half in range(2):
                src = biasT[h][half * 1024:(half + 1) * 1024, qs]
                p.dma(bias_str[:, half * 8:(half + 1) * 8, :],
                      src.rearrange("(kc p) q -> p kc q", p=128))
            for b in range(NB):
                for kc in range(16):
                    p.mm(pS[:, :], k4[:, b, kc * 128:(kc + 1) * 128],
                         q4[:, b, qs])
                    p.tt(es_f[:, kc, :], pS[:, :], bias_str[:, kc, :], OP.add)
                p.act(es_b[:, :, :], es_f[:, :, :], AF.Exp, scale=0.125)
                for kc in range(16):
                    p.mm(pA[0:65, :], v4[:, kc, b, :], es_b[:, kc, :],
                         start=(kc == 0), stop=(kc == 15), chain=True)
                p.recip(rden[:, :], pA[64:65, :])
                p.mm(pB[0:64, :], onesf[0:1, 0:64], rden[:, :])
                p.act(avf[:, :], pA[0:64, :], AF.Copy)
                p.tt(avb[:, :], avf[:, :], pB[0:64, :], OP.mult)
                p.dma(ao[b, h * 64:(h + 1) * 64, qs], avb[:, :])
    scrw = _sb(nc, st, [128, 512], F32)
    wo_bf = load_wbf(p, nc, st, wo, 8, 1024, scrw)
    bo_sb = _sb(nc, st, [128, 8], F32)
    p.dma(bo_sb[:, :], bo[:, :])
    as_sb = _sb(nc, st, [128, 8, 512], BF)
    prj = _sb(nc, st, [128, 512], F32)
    pP = _ps(nc, st)
    for b in range(NB):
        for qt in range(NQT):
            qs = slice(qt * 512, (qt + 1) * 512)
            for half in range(2):
                src = ao[b][half * 512:(half + 1) * 512, qs]
                p.dma(as_sb[:, half * 4:(half + 1) * 4, :],
                      src.rearrange("(c p) q -> p c q", p=128))
            for co in range(8):
                for k in range(8):
                    p.mm(pP[:, :], wo_bf[:, k, co * 128:(co + 1) * 128],
                         as_sb[:, k, :], start=(k == 0), stop=(k == 7),
                         chain=True)
                p.ts1(prj[:, :], pP[:, :], bo_sb[:, co:co + 1], OP.add)
                p.dma(aout[b, co * 128:(co + 1) * 128, qs], prj[:, :])


def emit_ffn0(nc, p, st, xin, attn, xnf, xnb, w1, w2, w3, w4,
              cb1, cb2, cb3, cb4, gw1, gw2, gw3, gb1, gb2, gb3, arin, x1):
    scrw = _sb(nc, st, [128, 512], F32)
    w1_bf = load_wbf(p, nc, st, w1, 8, 1024, scrw)
    w2_bf = load_wbf(p, nc, st, w2, 8, 512, scrw)
    w3_bf = load_wbf(p, nc, st, w3, 4, 256, scrw)
    gw1_bf = load_wbf(p, nc, st, gw1, 8, 512, scrw)
    gw2_bf = load_wbf(p, nc, st, gw2, 4, 256, scrw)
    w4_bf = _sb(nc, st, [128, 2, 1], BF)
    gw3_bf = _sb(nc, st, [128, 2, 1], BF)
    w4f = _sb(nc, st, [128, 2], F32)
    p.dma(w4f[:, :], w4.rearrange("(k p) o -> p (k o)", p=128), slow=True)
    p.cp(w4_bf[:, :, 0], w4f[:, :])
    p.dma(w4f[:, :], gw3.rearrange("(k p) o -> p (k o)", p=128), slow=True)
    p.cp(gw3_bf[:, :, 0], w4f[:, :])
    cb1_sb = _sb(nc, st, [128, 8], F32)
    cb2_sb = _sb(nc, st, [128, 4], F32)
    cb3_sb = _sb(nc, st, [128, 2], F32)
    cb4_sb = _sb(nc, st, [1, 1], F32)
    gb1_sb = _sb(nc, st, [128, 4], F32)
    gb2_sb = _sb(nc, st, [128, 2], F32)
    gb3_sb = _sb(nc, st, [1, 1], F32)
    ar_sb = _sb(nc, st, [1, 1], F32)
    for dst, src in [(cb1_sb, cb1), (cb2_sb, cb2), (cb3_sb, cb3),
                     (cb4_sb, cb4), (gb1_sb, gb1), (gb2_sb, gb2),
                     (gb3_sb, gb3), (ar_sb, arin)]:
        p.dma(dst[:, :], src[:, :])
    c01 = _sb(nc, st, [1, 1], F32)
    cm1 = _sb(nc, st, [1, 1], F32)
    ar2 = _sb(nc, st, [1, 1], F32)
    arc = _sb(nc, st, [1, 1], F32)
    p.memset(c01[:, :], 0.1)
    p.memset(cm1[:, :], -1.0)
    p.act(ar2[:, :], ar_sb[:, :], AF.Copy, scale=2.0)
    p.ts2(arc[:, :], ar_sb[:, :], cm1[:, :], c01[:, :], OP.mult, OP.add)
    onesf = _sb(nc, st, [1, 128], F32)
    p.memset(onesf[:, :], 1.0)
    xnb_sb = _sb(nc, st, [128, 8, T], BF)
    out_f = _sb(nc, st, [128, 8, T], F32)
    out_b = _sb(nc, st, [128, 8, T], BF)
    h1b = _sb(nc, st, [128, 8, 512], BF)
    h2b = _sb(nc, st, [128, 4, 512], BF)
    h3b = _sb(nc, st, [128, 2, 512], BF)
    tf_sb = _sb(nc, st, [1, 512], F32)
    cpl = _sb(nc, st, [1, 512], F32)
    gcf = _sb(nc, st, [128, 8, 1], F32)
    gcb = _sb(nc, st, [128, 8, 1], BF)
    g1f = _sb(nc, st, [128, 4, 1], F32)
    g1b = _sb(nc, st, [128, 4, 1], BF)
    g2f = _sb(nc, st, [128, 2, 1], F32)
    g2b = _sb(nc, st, [128, 2, 1], BF)
    gf_sb = _sb(nc, st, [1, 1], F32)
    gf7 = _sb(nc, st, [1, 1], F32)
    tA = _sb(nc, st, [128, 512], F32)
    tB = _sb(nc, st, [128, 512], F32)
    r1 = tA
    r2 = tB
    pC1 = _ps(nc, st)
    pC4 = _ps(nc, st)
    pBC = _ps(nc, st)
    pG = _ps(nc, st)
    for b in range(NB):
        for half in range(2):
            p.dma(xnb_sb[:, half * 4:(half + 1) * 4, :],
                  xnb[b][half * 512:(half + 1) * 512, :]
                  .rearrange("(c p) t -> p c t", p=128))
            p.dma(out_f[:, half * 4:(half + 1) * 4, :],
                  xnf[b][half * 512:(half + 1) * 512, :]
                  .rearrange("(c p) t -> p c t", p=128))
        for c in range(8):
            p.red(gcf[:, c, :], out_f[:, c, :])
        p.cp(gcb[:, :, :], gcf[:, :, :])
        for m in range(4):
            for k in range(8):
                p.mm(pG[:, 0:1], gw1_bf[:, k, m * 128:(m + 1) * 128],
                     gcb[:, k, :], start=(k == 0), stop=(k == 7), chain=True)
            p.act(g1f[:, m, :], pG[:, 0:1], AF.Gelu,
                  bias=gb1_sb[:, m:m + 1], scale=1.0 / 2048.0)
        p.cp(g1b[:, :, :], g1f[:, :, :])
        for m in range(2):
            for k in range(4):
                p.mm(pG[:, 0:1], gw2_bf[:, k, m * 128:(m + 1) * 128],
                     g1b[:, k, :], start=(k == 0), stop=(k == 3), chain=True)
            p.act(g2f[:, m, :], pG[:, 0:1], AF.Gelu,
                  bias=gb2_sb[:, m:m + 1])
        p.cp(g2b[:, :, :], g2f[:, :, :])
        for k in range(2):
            p.mm(pG[0:1, 0:1], gw3_bf[:, k, :], g2b[:, k, :],
                 start=(k == 0), stop=(k == 1), chain=True)
        p.act(gf_sb[:, :], pG[0:1, 0:1], AF.Sigmoid, bias=gb3_sb[:, :])
        p.act(gf7[:, :], gf_sb[:, :], AF.Copy, scale=0.7)
        for cyc in range(3):
            rhs = xnb_sb if cyc == 0 else out_b
            for qt in range(NQT):
                qs = slice(qt * 512, (qt + 1) * 512)
                for m in range(8):
                    for k in range(8):
                        p.mm(pC1[:, :], w1_bf[:, k, m * 128:(m + 1) * 128],
                             rhs[:, k, qs], start=(k == 0), stop=(k == 7),
                             chain=True)
                    p.act(h1b[:, m, :], pC1[:, :], AF.Gelu,
                          bias=cb1_sb[:, m:m + 1])
                for m in range(4):
                    for k in range(8):
                        p.mm(pC1[:, :], w2_bf[:, k, m * 128:(m + 1) * 128],
                             h1b[:, k, :], start=(k == 0), stop=(k == 7),
                             chain=True)
                    p.act(h2b[:, m, :], pC1[:, :], AF.Gelu,
                          bias=cb2_sb[:, m:m + 1])
                for m in range(2):
                    for k in range(4):
                        p.mm(pC1[:, :], w3_bf[:, k, m * 128:(m + 1) * 128],
                             h2b[:, k, :], start=(k == 0), stop=(k == 3),
                             chain=True)
                    p.act(h3b[:, m, :], pC1[:, :], AF.Gelu,
                          bias=cb3_sb[:, m:m + 1])
                for k in range(2):
                    p.mm(pC4[0:1, :], w4_bf[:, k, :], h3b[:, k, :],
                         start=(k == 0), stop=(k == 1), chain=True)
                p.act(tf_sb[:, :], pC4[0:1, :], AF.Sigmoid, bias=cb4_sb[:, :])
                p.act(tf_sb[:, :], tf_sb[:, :], AF.Copy, scale=0.3)
                p.ts1(tf_sb[:, :], tf_sb[:, :], gf7[:, :], OP.add)
                p.ts2(cpl[:, :], tf_sb[:, :], ar2[:, :], arc[:, :],
                      OP.mult, OP.add)
                p.mm(pBC[:, :], onesf[0:1, :], cpl[:, :])
                for (a, bb, sa) in [(0, 4, 1), (1, 5, 1), (2, 6, -1),
                                    (3, 7, -1)]:
                    p.tt(tA[:, :], out_f[:, bb, qs], pBC[:, :], OP.mult)
                    p.tt(tB[:, :], out_f[:, a, qs], pBC[:, :], OP.mult)
                    p.tt(out_f[:, a, qs], out_f[:, a, qs], tA[:, :],
                         OP.add if sa == 1 else OP.subtract)
                    p.tt(out_f[:, bb, qs], out_f[:, bb, qs], tB[:, :],
                         OP.subtract if sa == 1 else OP.add)
                if cyc < 2:
                    for c in range(8):
                        p.act(out_b[:, c, qs], out_f[:, c, qs], AF.Copy)
        for c in range(8):
            for qt in range(NQT):
                qs = slice(qt * 512, (qt + 1) * 512)
                p.dma(r1[:, :], xin[b, c * 128:(c + 1) * 128, qs])
                p.dma(r2[:, :], attn[b, c * 128:(c + 1) * 128, qs])
                p.tt(r1[:, :], r1[:, :], out_f[:, c, qs], OP.add)
                p.tt(r1[:, :], r1[:, :], r2[:, :], OP.add)
                p.dma(x1[b, c * 128:(c + 1) * 128, qs], r1[:, :])


def emit_ffn1(nc, p, st, x1, attn, xnb, wg, wu, wd, yout, hh):
    st1 = ExitStack()
    wint = _sb(nc, st1, [128, 3072], I32)
    wf = _sb(nc, st1, [128, 3072], F32)
    wg_bf = _sb(nc, st1, [128, 8, 3072], BF)
    wu_bf = _sb(nc, st1, [128, 8, 3072], BF)
    xnb_sb = _sb(nc, st1, [128, 8, T], BF)
    t1 = _sb(nc, st1, [128, 512], F32)
    hscr = _sb(nc, st1, [128, 512], BF)
    for k in range(8):
        p.dma(wint[:, :], wg[k * 128:(k + 1) * 128, :])
        p.cp(wf[:, :], wint[:, :])
        p.cp(wg_bf[:, k, :], wf[:, :])
        p.dma(wint[:, :], wu[k * 128:(k + 1) * 128, :])
        p.cp(wf[:, :], wint[:, :])
        p.cp(wu_bf[:, k, :], wf[:, :])
    pG1 = _ps(nc, st1)
    pU1 = _ps(nc, st1)
    for b in range(NB):
        for half in range(2):
            p.dma(xnb_sb[:, half * 4:(half + 1) * 4, :],
                  xnb[b][half * 512:(half + 1) * 512, :]
                  .rearrange("(c p) t -> p c t", p=128))
        for qt in range(NQT):
            qs = slice(qt * 512, (qt + 1) * 512)
            for mc in range(24):
                for k in range(8):
                    p.mm(pG1[:, :], wg_bf[:, k, mc * 128:(mc + 1) * 128],
                         xnb_sb[:, k, qs], start=(k == 0), stop=(k == 7),
                         chain=True)
                for k in range(8):
                    p.mm(pU1[:, :], wu_bf[:, k, mc * 128:(mc + 1) * 128],
                         xnb_sb[:, k, qs], start=(k == 0), stop=(k == 7),
                         chain=True)
                p.act(t1[:, :], pG1[:, :], AF.Silu, scale=1.0 / 127.0)
                p.tt(hscr[:, :], t1[:, :], pU1[:, :], OP.mult)
                p.dma(hh[b, mc * 128:(mc + 1) * 128, qs], hscr[:, :])
    st1.close()
    wint2 = _sb(nc, st, [128, 1024], I32)
    wf2 = _sb(nc, st, [128, 1024], F32)
    wd_bf = _sb(nc, st, [128, 24, 1024], BF)
    hs = _sb(nc, st, [128, 24, 512], BF)
    yscr = _sb(nc, st, [128, 512], F32)
    r1 = _sb(nc, st, [128, 512], F32)
    r2 = _sb(nc, st, [128, 512], F32)
    for k in range(24):
        p.dma(wint2[:, :], wd[k * 128:(k + 1) * 128, :])
        p.cp(wf2[:, :], wint2[:, :])
        p.cp(wd_bf[:, k, :], wf2[:, :])
    pD = _ps(nc, st)
    for b in range(NB):
        for qt in range(NQT):
            qs = slice(qt * 512, (qt + 1) * 512)
            for third in range(3):
                src = hh[b][third * 1024:(third + 1) * 1024, qs]
                p.dma(hs[:, third * 8:(third + 1) * 8, :],
                      src.rearrange("(m p) q -> p m q", p=128))
            for co in range(8):
                for k in range(24):
                    p.mm(pD[:, :], wd_bf[:, k, co * 128:(co + 1) * 128],
                         hs[:, k, :], start=(k == 0), stop=(k == 23),
                         chain=True)
                p.act(yscr[:, :], pD[:, :], AF.Copy, scale=1.0 / 16129.0)
                p.dma(r1[:, :], x1[b, co * 128:(co + 1) * 128, qs])
                p.dma(r2[:, :], attn[b, co * 128:(co + 1) * 128, qs])
                p.tt(yscr[:, :], yscr[:, :], r1[:, :], OP.add)
                p.tt(yscr[:, :], yscr[:, :], r2[:, :], OP.add)
                p.dma(yout[b, co * 128:(co + 1) * 128, qs], yscr[:, :])


def build_all():
    nc = bass.Bass("TRN2", target_bir_lowering=False)
    st = ExitStack()

    def ein(name, shape, dt):
        return nc.dram_tensor(name, shape, dt, kind="ExternalInput").ap()

    def internal(name, shape, dt):
        return nc.dram_tensor(name, shape, dt, kind="Internal").ap()

    x = ein("x", [NB, D, T], F32)
    wqk0 = ein("wqk0", [D, 2048], F32)
    wv0 = ein("wv0", [D, D], F32)
    wo0 = ein("wo0", [D, D], F32)
    maskc = ein("maskc", [4, 128, 512], F32)
    w1 = ein("w1", [1024, 1024], F32)
    w2 = ein("w2", [1024, 512], F32)
    w3 = ein("w3", [512, 256], F32)
    w4 = ein("w4", [256, 1], F32)
    cb1 = ein("cb1", [128, 8], F32)
    cb2 = ein("cb2", [128, 4], F32)
    cb3 = ein("cb3", [128, 2], F32)
    cb4 = ein("cb4", [1, 1], F32)
    gw1 = ein("gw1", [1024, 512], F32)
    gw2 = ein("gw2", [512, 256], F32)
    gw3 = ein("gw3", [256, 1], F32)
    gb1 = ein("gb1", [128, 4], F32)
    gb2 = ein("gb2", [128, 2], F32)
    gb3 = ein("gb3", [1, 1], F32)
    arin = ein("ar", [1, 1], F32)
    wqk1 = ein("wqk1", [D, 2048], F32)
    wv1 = ein("wv1", [D, D], F32)
    bqk1 = ein("bqk1", [128, 16], F32)
    bv1 = ein("bv1", [1, D], F32)
    wo1 = ein("wo1", [D, D], F32)
    bo1 = ein("bo1", [128, 8], F32)
    rvec = ein("rvec", [1, 4095], F32)
    w1r = ein("w1r", [1, 32], F32)
    b1r = ein("b1r", [32, 1], F32)
    w2r = ein("w2r", [32, 16], F32)
    b2r = ein("b2r", [16, 1], F32)
    wg = ein("wg", [1024, 3072], I32)
    wu = ein("wu", [1024, 3072], I32)
    wd = ein("wd", [3072, 1024], I32)
    y = nc.dram_tensor("y", [NB, D, T], F32, kind="ExternalOutput").ap()

    xnf0 = internal("xnf0", [NB, D, T], F32)
    xnb0 = internal("xnb0", [NB, D, T], BF)
    qk0 = internal("qk0", [NB, 2048, T], BF)
    v0 = internal("v0", [NB, T, D], BF)
    at0 = internal("at0", [NB, D, T], F32)
    x1d = internal("x1d", [NB, D, T], F32)
    xnf1 = internal("xnf1", [NB, D, T], F32)
    xnb1 = internal("xnb1", [NB, D, T], BF)
    qk1 = internal("qk1", [NB, 2048, T], BF)
    v1 = internal("v1", [NB, T, D], BF)
    biasT = internal("biasT", [NH, T, T], F32)
    ao = internal("ao", [NB, D, T], BF)
    at1 = internal("at1", [NB, D, T], F32)
    hh = internal("hh", [NB, 3072, T], BF)

    p = Prog(nc)
    stages = [
        lambda s: emit_ln(nc, p, s, x, xnf0, xnb0),
        lambda s: emit_qkv(nc, p, s, 0, xnb0, wqk0, wv0, qk0, v0),
        lambda s: emit_attn0(nc, p, s, qk0, v0, wo0, maskc, at0),
        lambda s: emit_ffn0(nc, p, s, x, at0, xnf0, xnb0, w1, w2, w3, w4,
                            cb1, cb2, cb3, cb4, gw1, gw2, gw3, gb1, gb2,
                            gb3, arin, x1d),
        lambda s: emit_ln(nc, p, s, x1d, xnf1, xnb1),
        lambda s: emit_qkv(nc, p, s, 1, xnb1, wqk1, wv1, qk1, v1,
                           bqk=bqk1, bv=bv1),
        lambda s: emit_attn1(nc, p, s, qk1, v1, wo1, bo1, rvec, w1r, b1r,
                             w2r, b2r, at1, biasT, ao),
        lambda s: emit_ffn1(nc, p, s, x1d, at1, xnb1, wg, wu, wd, y, hh),
    ]
    for stg in stages:
        sub = ExitStack()
        stg(sub)
        sub.close()
    p.finalize(st)
    st.close()
    return nc


_PROGS = {}


def _bias_cols(b):
    n = b.shape[0]
    return np.ascontiguousarray(b.reshape(n // 128, 128).T).astype(np.float32)


def kernel(**ins):
    f32 = np.float32
    x = np.asarray(ins["L0_attn_x"], f32)
    cores = list(range(8))
    xT = [np.ascontiguousarray(x[4 * i:4 * i + 4].transpose(0, 2, 1))
          for i in cores]

    Wqkv0 = np.asarray(ins["L0_attn_W_qkv"], f32)
    wqk0 = np.ascontiguousarray(Wqkv0[:2048].T)
    wv0 = np.ascontiguousarray(Wqkv0[2048:].T)
    wo0 = np.ascontiguousarray(np.asarray(ins["L0_attn_W_out"], f32).T)
    ii = np.arange(128)[:, None]
    jj = np.arange(512)[None, :]
    maskc = np.stack([np.where(jj >= ii + 128 * m, 0.0, -1e9)
                      for m in range(4)]).astype(f32)
    maskc = np.ascontiguousarray(maskc)

    wqkv1 = np.asarray(ins["L1_attn_w_qkv"], f32)
    wqk1 = np.ascontiguousarray(wqkv1[:, :2048])
    wv1 = np.ascontiguousarray(wqkv1[:, 2048:])
    bqkv1 = np.asarray(ins["L1_attn_b_qkv"], f32)
    bqk1 = _bias_cols(bqkv1[:2048])
    bv1 = np.ascontiguousarray(bqkv1[2048:].reshape(1, 1024))
    wo1 = np.ascontiguousarray(np.asarray(ins["L1_attn_w_out"], f32))
    bo1 = _bias_cols(np.asarray(ins["L1_attn_b_out"], f32))
    rel = np.arange(4095, dtype=np.float64) - 2047.0
    rvec = (np.sign(rel) * np.log1p(np.abs(rel))).astype(f32).reshape(1, 4095)
    w1r = np.ascontiguousarray(np.asarray(ins["L1_attn_w1_rpe"], f32))
    b1r = np.ascontiguousarray(
        np.asarray(ins["L1_attn_b1_rpe"], f32).reshape(32, 1))
    w2r = np.ascontiguousarray(np.asarray(ins["L1_attn_w2_rpe"], f32))
    b2r = np.ascontiguousarray(
        np.asarray(ins["L1_attn_b2_rpe"], f32).reshape(16, 1))

    wgT = np.ascontiguousarray(np.asarray(ins["L1_ffn_w_gate"]).T).astype(np.int32)
    wuT = np.ascontiguousarray(np.asarray(ins["L1_ffn_w_up"]).T).astype(np.int32)
    wdT = np.ascontiguousarray(np.asarray(ins["L1_ffn_w_down"]).T).astype(np.int32)

    shared = {
        "wqk0": wqk0, "wv0": wv0, "wo0": wo0, "maskc": maskc,
        "w1": np.ascontiguousarray(np.asarray(ins["L0_ffn_cn_w1"], f32)),
        "w2": np.ascontiguousarray(np.asarray(ins["L0_ffn_cn_w2"], f32)),
        "w3": np.ascontiguousarray(np.asarray(ins["L0_ffn_cn_w3"], f32)),
        "w4": np.ascontiguousarray(np.asarray(ins["L0_ffn_cn_w4"], f32)),
        "cb1": _bias_cols(np.asarray(ins["L0_ffn_cn_b1"], f32)),
        "cb2": _bias_cols(np.asarray(ins["L0_ffn_cn_b2"], f32)),
        "cb3": _bias_cols(np.asarray(ins["L0_ffn_cn_b3"], f32)),
        "cb4": np.asarray(ins["L0_ffn_cn_b4"], f32).reshape(1, 1),
        "gw1": np.ascontiguousarray(np.asarray(ins["L0_ffn_gc_w1"], f32)),
        "gw2": np.ascontiguousarray(np.asarray(ins["L0_ffn_gc_w2"], f32)),
        "gw3": np.ascontiguousarray(np.asarray(ins["L0_ffn_gc_w3"], f32)),
        "gb1": _bias_cols(np.asarray(ins["L0_ffn_gc_b1"], f32)),
        "gb2": _bias_cols(np.asarray(ins["L0_ffn_gc_b2"], f32)),
        "gb3": np.asarray(ins["L0_ffn_gc_b3"], f32).reshape(1, 1),
        "ar": np.asarray(ins["L0_ffn_adaptive_range"], f32).reshape(1, 1),
        "wqk1": wqk1, "wv1": wv1, "bqk1": bqk1, "bv1": bv1,
        "wo1": wo1, "bo1": bo1, "rvec": rvec, "w1r": w1r, "b1r": b1r,
        "w2r": w2r, "b2r": b2r, "wg": wgT, "wu": wuT, "wd": wdT,
    }
    if "all" not in _PROGS:
        _PROGS["all"] = build_all()
    nc = _PROGS["all"]
    in_maps = [dict(shared, x=xT[i]) for i in cores]
    res = run_bass_kernel_spmd(nc, in_maps, core_ids=cores)
    y = [np.asarray(res.results[i]["y"], f32).transpose(0, 2, 1)
         for i in cores]
    return np.ascontiguousarray(np.concatenate(y, axis=0))
